# version 25
# baseline (speedup 1.0000x reference)
import contextlib
import numpy as np
import ml_dtypes
import concourse.bass as bass
import concourse.mybir as mybir
from concourse.bass_utils import run_bass_kernel_spmd

F32 = mybir.dt.float32
BF16 = mybir.dt.bfloat16
AF = mybir.ActivationFunctionType
AL = mybir.AluOpType
AX = mybir.AxisListType

T, E, FF = 2048, 1024, 4096
NCH = 8
TCH = 512
NT = T // TCH
NKT = T // 128
NFC = FF // 128
NEG = -1.0e30
VBLK = 160
VROW = 8 * VBLK

ENGS = ("pe", "act", "dve", "sync", "gp")


class Em:

    def __init__(self):
        self.ops = {e: [] for e in ENGS}
        self.cnt = {}
        self.lastw = {}

    def op(self, eng, fn):
        self.ops[eng].append(("op", fn))

    def inc(self, eng, sem, fn, step=1):
        self.cnt[sem] = self.cnt.get(sem, 0) + step
        tok = self.cnt[sem]
        self.ops[eng].append(("inc", fn, sem, step))
        return (sem, tok)

    def wait(self, eng, tok):
        if tok is None:
            return
        sem, val = tok
        if val <= 0:
            return
        assert val <= self.cnt.get(sem, 0), f"wait on future inc: {eng} {sem} {val}"
        key = (eng, sem)
        if self.lastw.get(key, 0) >= val:
            return
        self.lastw[key] = val
        self.ops[eng].append(("wait", sem, val))

    def run(self, eng, engine, sems):
        for it in self.ops[eng]:
            if it[0] == "op":
                it[1](engine)
            elif it[0] == "inc":
                it[1](engine).then_inc(sems[it[2]], it[3])
            else:
                engine.wait_ge(sems[it[1]], it[2])


def build(debug=False, phases=3, p2cut=None, p1cut=None):
    nc = bass.Bass("TRN2", target_bir_lowering=False, detect_race_conditions=False)

    dx = nc.dram_tensor("x", [E, T], F32, kind="ExternalInput").ap()
    dwq = nc.dram_tensor("wq", [E, E], F32, kind="ExternalInput").ap()
    dwk = nc.dram_tensor("wk", [E, E], F32, kind="ExternalInput").ap()
    dwv = nc.dram_tensor("wv", [E, E], F32, kind="ExternalInput").ap()
    dwo = nc.dram_tensor("wo", [E, E], F32, kind="ExternalInput").ap()
    dwfc = nc.dram_tensor("wfc", [E, FF], F32, kind="ExternalInput").ap()
    dwpj = nc.dram_tensor("wpj", [FF, E], F32, kind="ExternalInput").ap()
    dcos = nc.dram_tensor("cosT", [E, T], F32, kind="ExternalInput").ap()
    dsin = nc.dram_tensor("sinT", [E, T], F32, kind="ExternalInput").ap()
    dbo = nc.dram_tensor("bo", [128, 8], F32, kind="ExternalInput").ap()
    dcf = nc.dram_tensor("cf", [128, 288], F32, kind="ExternalInput").ap()
    dcb = nc.dram_tensor("cb", [128, 392], BF16, kind="ExternalInput").ap()
    dout = nc.dram_tensor("out", [E, T], F32, kind="ExternalOutput").ap()
    dwfcb = nc.dram_tensor("wfc_b", [E, FF], BF16, kind="Internal").ap()
    dwpjb = nc.dram_tensor("wpj_b", [FF, E], BF16, kind="Internal").ap()
    skind = "ExternalOutput" if debug else "Internal"
    dqs = nc.dram_tensor("q_s", [128, NCH * T], BF16, kind=skind).ap()
    dxns = nc.dram_tensor("xn_s", [128, NCH * T], BF16, kind=skind).ap()
    das = nc.dram_tensor("a_s", [128, NCH * T], BF16, kind=skind).ap()
    if debug:
        dkdbg = nc.dram_tensor("k_dbg", [128, NCH * T], BF16, kind="ExternalOutput").ap()
        dvdbg = nc.dram_tensor("v_dbg", [128, NKT * VROW], BF16, kind="ExternalOutput").ap()
        dzdbg = nc.dram_tensor("z_dbg", [128, 64], F32, kind="ExternalOutput").ap()
        dofm = nc.dram_tensor("ofm_dbg", [128, 4 * 4096], BF16, kind="ExternalOutput").ap()
        dpt = nc.dram_tensor("pt_dbg", [128, 8 * 1024], BF16, kind="ExternalOutput").ap()
        dnegc = nc.dram_tensor("negc_dbg", [128, 8], F32, kind="ExternalOutput").ap()

    SB_BASE = 16512
    SB_TOP = 229376

    def sb(name, shape, dtype, off):
        return nc.alloc_sbuf_tensor_at(name, shape, dtype, offset=SB_BASE + ((off + 31) & ~31))

    o = 0
    cf = sb("cf", [128, 288], F32, o); o += 1184
    cb = sb("cb", [128, 392], BF16, o); o += 800
    bo_sb = sb("bo_sb", [128, 8], F32, o); o += 32
    negc = sb("negc", [128, 8], F32, o); o += 32
    zcol = sb("zcol", [128, 1], F32, o); o += 32
    mq = sb("mq", [1, 32], F32, o); o += 128
    mk_ = sb("mk", [1, 32], F32, o); o += 128
    mqf = sb("mqf", [1, 8], F32, o); o += 32
    mkf = sb("mkf", [1, 8], F32, o); o += 32
    ncr = sb("ncr", [1, 8], F32, o); o += 32
    ncr2 = sb("ncr2", [1, 8], F32, o); o += 32
    ncrb = sb("ncrb", [1, 8], BF16, o); o += 32
    a_s = sb("a_s2", [128, 512], BF16, o); o += 1024
    b_s = sb("b_s2", [128, 512], BF16, o); o += 1024
    tbufb = sb("tbufb", [128, 512], BF16, o); o += 1024
    var_s = sb("var_s", [128, 512], F32, o); o += 2048
    mu_s = sb("mu_s", [128, 512], F32, o); o += 2048
    zrow = sb("zrow", [128, 1024], F32, o); o += 4096
    zinv = sb("zinv", [128, 2048], BF16, o); o += 4096
    osb = sb("osb", [128, 2 * 512], F32, o); o += 4096
    y_sb = sb("y_sb", [128, 2 * 512], BF16, o); o += 2048
    mem = sb("mem", [128, 512], BF16, o); o += 2048
    spk = sb("spk", [128, 512], BF16, o); o += 2048
    accs = sb("accs", [128, 512], BF16, o); o += 2048
    xin = sb("xin", [128, 2 * 512], F32, o); o += 4096
    ain = sb("ain", [128, 2 * 512], BF16, o); o += 2048
    outb_off = o
    outb = sb("outb", [128, 2 * 512], F32, o); o += 4096
    rsb = sb("rsb", [128, 2048], BF16, o); o += 4096
    qsqb = sb("qsqb", [128, 4 * 512], BF16, o); o += 4096
    tbuf = sb("tbuf", [128, 512], F32, outb_off)
    SMALL_END = o

    k_sb = sb("k_sb", [128, NCH * T], BF16, o); o += 32768
    v_ext = sb("v_ext", [128, NKT * VROW], BF16, o); o += NKT * VROW * 2
    wo_sb = sb("wo_sb", [128, 8192], BF16, o); o += 16384
    P12_END = o

    P1_BASE = o
    xb = sb("xb", [128, 2 * 4096], BF16, o); o += 16384
    x2 = sb("x2", [128, 4 * 512], BF16, o); o += 4096
    cosb = sb("cosb", [128, 4 * 512], BF16, o); o += 4096
    sinb = sb("sinb", [128, 4 * 512], BF16, o); o += 4096
    ub = sb("ub", [128, 2 * 512], BF16, o); o += 2048
    wbuf = sb("wbuf", [128, 2 * 512], BF16, o); o += 2048
    rxb = sb("rxb", [128, 4096], BF16, o); o += 8192
    wstr = sb("wstr", [128, 8 * 1024], BF16, o); o += 16384
    qstage = sb("qstage", [128, 4096], BF16, o); o += 8192
    xnb = sb("xnb", [128, 4096], BF16, o); o += 8192
    P1_END = o
    assert P1_END + SB_BASE <= SB_TOP, P1_END

    o2 = P1_BASE
    qQ = sb("qQ", [128, 2 * 4096], BF16, o2); o2 += 16384
    PT = sb("PT", [128, NKT * 1024], BF16, o2); o2 += NKT * 1024 * 2
    Ofm = sb("Ofm", [128, 4096], BF16, o2); o2 += 8192
    Astage = sb("Astage", [128, 4096], BF16, o2); o2 += 8192
    assert o2 + SB_BASE <= SB_TOP, o2

    o3 = SMALL_END
    wpj_sb = sb("wpj_sb", [128, NFC * 1024], BF16, o3); o3 += NFC * 1024 * 2
    wfc_sb = sb("wfc_sb", [128, 2 * 4096], BF16, o3); o3 += 16384
    xn3 = sb("xn3", [128, 2 * 4096], BF16, o3); o3 += 16384
    h_sb = sb("h_sb", [128, 2 * 16384], BF16, o3); o3 += 65536
    assert o3 + SB_BASE <= SB_TOP, o3

    pA = nc.alloc_psum_tensor("pA", [128, 2048], F32)
    pB = nc.alloc_psum_tensor("pB", [128, 2048], F32)
    BANKS = [pA[:, 512 * i:512 * (i + 1)] for i in range(4)] + \
            [pB[:, 512 * i:512 * (i + 1)] for i in range(4)]

    em = Em()

    def MM(out, l, r, st, sp, tp=None):
        return lambda En: En.matmul(out, l, r, start=st, stop=sp, tile_position=tp)

    def DMA(dst, src):
        return lambda En: En.dma_start(dst, src)

    def ACTI(out, in_, func, bias, scale=1.0, accum=None):
        return lambda En: En.activation(out, in_, func, bias=bias, scale=scale, accum_out=accum)

    def TT(opk, out, a, b):
        return lambda En: En.tensor_tensor(out, a, b, opk)

    def TS(out, a, s1, opk):
        return lambda En: En.tensor_scalar(out, a, s1, None, opk)

    def STT(out, a, s, b, op0, op1):
        return lambda En: En.scalar_tensor_tensor(out, a, s, b, op0, op1)

    def CP(out, in_):
        return lambda En: En.tensor_copy(out, in_)

    def RMAX(out, in_):
        return lambda En: En.reduce_max(out, in_, axis=AX.X)

    SHUF_MASK = [i ^ 1 for i in range(32)]

    def SHUF(out, in_):
        return lambda En: En.stream_shuffle(out, in_, SHUF_MASK)

    def RECIP(out, in_):
        return lambda En: En.reciprocal(out, in_)

    def MEMSET(ap, v):
        return lambda En: En.memset(ap, v)

    def NOP():
        return lambda En: En.nop()

    def bar(_name):
        toks = [(s, c) for s, c in em.cnt.items() if c > 0]
        for e in ENGS:
            for t in toks:
                em.wait(e, t)

    bankfree = {}

    def use_bank(eng, i):
        em.wait(eng, bankfree.get(i))

    em.inc("sync", "cst", DMA(cf[:, :], dcf[:, :]), 16)
    em.inc("sync", "cst", DMA(cb[:, :], dcb[:, :]), 16)
    t_cst = em.inc("sync", "cst", DMA(bo_sb[:, :], dbo[:, :]), 16)
    em.wait("pe", t_cst)
    em.wait("act", t_cst)
    em.wait("dve", t_cst)
    em.inc("dve", "dve", MEMSET(zcol[:, :], 0.0))
    em.inc("dve", "dve", MEMSET(zrow[:, :], 0.0))
    em.inc("dve", "dve", MEMSET(v_ext[:, :], 0.0))
    ones_ap = bass.AP(v_ext, 64, [[NKT * VROW, 128], [VROW, NKT], [VBLK, 8]])
    t_vinit = em.inc("dve", "dve", MEMSET(ones_ap, 1.0))

    def xload(tt):
        par = tt % 2
        toks = []
        for c in range(NCH):
            toks.append(em.inc("gp", f"gx{par}",
                               DMA(xb[:, 4096 * par + 512 * c: 4096 * par + 512 * (c + 1)],
                                   dx[128 * c:128 * (c + 1), TCH * tt:TCH * (tt + 1)]), 16))
        return toks

    def tload(tt, c):
        par = c % 4
        em.inc("gp", f"gt{par}", DMA(cosb[:, 512 * par:512 * (par + 1)],
                                     dcos[128 * c:128 * (c + 1), TCH * tt:TCH * (tt + 1)]), 16)
        return em.inc("gp", f"gt{par}", DMA(sinb[:, 512 * par:512 * (par + 1)],
                                            dsin[128 * c:128 * (c + 1), TCH * tt:TCH * (tt + 1)]), 16)

    def wstream(buf, src, c, par, sem):
        return em.inc("gp", f"{sem}{par}",
                      DMA(buf[:, 1024 * par:1024 * (par + 1)], src[128 * c:128 * (c + 1), :]), 16)

    def wo_load():
        tok = None
        for c in range(NCH):
            tok = em.inc("gp", "gwo", DMA(wo_sb[:, 1024 * c:1024 * (c + 1)],
                                          dwo[128 * c:128 * (c + 1), :]), 16)
        return tok

    def wo_load():
        tok = None
        for c in range(NCH):
            tok = em.inc("gp", "gwo", DMA(wo_sb[:, 1024 * c:1024 * (c + 1)],
                                          dwo[128 * c:128 * (c + 1), :]), 16)
        return tok

    tok_xload = {0: xload(0)}
    tok_wo = wo_load()

    MU = BANKS[0]
    M2 = BANKS[1]

    tok_xn_done = {}
    tok_qs_wr = {}
    tok_xns_wr = {}
    tok_tbl_rx = {}
    wpar = 0
    tok_wstr_rd = {}

    for tt in range(NT):
        par = tt % 2
        xv = xb[:, 4096 * par:4096 * (par + 1)]
        use_bank("pe", 0)
        tok_mu = None
        for c in range(NCH):
            em.wait("pe", tok_xload[tt][c])
            fn = MM(MU, cb[:, 258:386], xv[:, 512 * c:512 * (c + 1)], c == 0, c == NCH - 1)
            if c == NCH - 1:
                tok_mu = em.inc("pe", "pe", fn)
            else:
                em.op("pe", fn)
        use_bank("pe", 1)
        tok_x2pe = {}
        for c in range(NCH):
            p2 = c % 4
            em.wait("dve", tok_x2pe.get(c - 4))
            em.wait("dve", tok_xload[tt][c])
            xc = xv[:, 512 * c:512 * (c + 1)]
            tok_x2 = em.inc("dve", "dve", TT(AL.mult, x2[:, 512 * p2:512 * (p2 + 1)], xc, xc))
            em.wait("pe", tok_x2)
            tok_x2pe[c] = em.inc("pe", "pe", MM(M2, cb[:, 258:386], x2[:, 512 * p2:512 * (p2 + 1)],
                                                c == 0, c == NCH - 1))
        tok_m2 = tok_x2pe[NCH - 1]
        em.wait("dve", tok_mu)
        em.wait("dve", tok_m2)
        em.inc("dve", "dve", CP(mu_s[:, :], MU))
        em.inc("dve", "dve", TT(AL.mult, tbuf[:, :], MU, mu_s[:, :]))
        tok_var = em.inc("dve", "dve", TT(AL.subtract, var_s[:, :], M2, tbuf[:, :]))
        bankfree[1] = tok_var
        em.wait("act", tok_var)
        em.inc("act", "act", ACTI(var_s[:, :], var_s[:, :], AF.Ln, cf[:, 256:257]))
        tok_a = em.inc("act", "act", ACTI(a_s[:, :], var_s[:, :], AF.Exp, zcol[:, 0:1], -0.5))
        em.wait("dve", tok_a)
        tok_b = em.inc("dve", "dve", TT(AL.mult, b_s[:, :], MU, a_s[:, :]))
        bankfree[0] = tok_b
        tok_xn = None
        for c in range(NCH):
            xc = xv[:, 512 * c:512 * (c + 1)]
            em.inc("dve", "dve", TT(AL.mult, tbufb[:, :], xc, a_s[:, :]))
            tok_xn = em.inc("dve", "dve", TT(AL.subtract, xnb[:, 512 * c:512 * (c + 1)], tbufb[:, :], b_s[:, :]))
        tok_xn_done[tt] = tok_xn
        em.wait("sync", tok_xn)
        em.wait("sync", tok_xns_wr.get(tt - 1))
        tok_xns_wr[tt] = em.inc("sync", "sxw", DMA(dxns[:, 4096 * tt:4096 * (tt + 1)], xnb[:, :]), 16)

        if p1cut == 1 and tt == 0:
            bar("c1"); return _realize(nc, em)
        em.wait("pe", t_vinit)
        em.wait("pe", tok_xn)
        for c in range(NCH):
            wp = wpar % 8; wpar += 1
            em.wait("gp", tok_wstr_rd.get(wpar - 9))
            tok_w = wstream(wstr, dwv, c, wp, "gw")
            em.wait("pe", tok_w)
            tok_v = None
            for s in range(4):
                for dh in range(2):
                    b = s * 2 + dh
                    if c == 0:
                        use_bank("pe", b)
                    tok_v = em.inc("pe", "pe",
                                   MM(BANKS[b], xnb[:, 512 * c + 128 * s: 512 * c + 128 * (s + 1)],
                                      wstr[:, 1024 * wp + 512 * dh: 1024 * wp + 512 * (dh + 1)],
                                      c == 0, c == NCH - 1))
            tok_wstr_rd[wpar - 1] = tok_v
        tok_rx_last = None
        for c in range(NCH):
            p2 = c % 4
            em.wait("gp", tok_tbl_rx.get((tt, c - 4)))
            tok_tb = tload(tt, c)
            em.wait("dve", tok_tb)
            u2 = c % 2
            xc = xnb[:, 512 * c:512 * (c + 1)]
            em.inc("dve", "dve", TT(AL.mult, wbuf[:, 512 * u2:512 * (u2 + 1)], xc, sinb[:, 512 * p2:512 * (p2 + 1)]))
            tok_uw = em.inc("dve", "dve", TT(AL.mult, ub[:, 512 * u2:512 * (u2 + 1)], xc, cosb[:, 512 * p2:512 * (p2 + 1)]))
            x2s = x2[:, 512 * (c % 4):512 * (c % 4 + 1)]
            em.inc("dve", "dve", SHUF(x2s, wbuf[:, 512 * u2:512 * (u2 + 1)]))
            tok_rx_last = em.inc("dve", "dve", TT(AL.add, rxb[:, 512 * c:512 * (c + 1)],
                                                  ub[:, 512 * u2:512 * (u2 + 1)], x2s))
            tok_tbl_rx[(tt, c)] = tok_uw
        for dd in range(1, 5):
            tok_tbl_rx[(tt + 1, -dd)] = tok_tbl_rx[(tt, NCH - dd)]

        for s in range(4):
            for dh in range(2):
                b = s * 2 + dh
                em.wait("dve", tok_v)
                kt = 4 * tt + s
                p0 = 4 * dh
                pt_, poff = (pA, 512 * b) if b < 4 else (pB, 512 * (b - 4))
                dst_e = bass.AP(v_ext, kt * VROW + p0 * VBLK, [[NKT * VROW, 128], [VBLK, 4], [1, 64]])
                src_e = bass.AP(pt_, poff, [[2048, 128], [128, 4], [1, 64]])
                em.inc("dve", "dve", CP(dst_e, src_e))
                dst_o = bass.AP(v_ext, kt * VROW + p0 * VBLK + 96, [[NKT * VROW, 128], [VBLK, 4], [1, 64]])
                src_o = bass.AP(pt_, poff + 64, [[2048, 128], [128, 4], [1, 64]])
                tok_ve = em.inc("dve", "dve", CP(dst_o, src_o))
                bankfree[b] = tok_ve

        if p1cut == 2 and tt == 0:
            bar("c2"); return _realize(nc, em)

        if p1cut == 3 and tt == 0:
            bar("c3"); return _realize(nc, em)
        em.wait("pe", tok_rx_last)
        em.wait("act", tok_qs_wr.get(tt - 1))
        tok_qev = None
        sq_jobs = []
        for which, src, dsts in (("q", dwq, None), ("k", dwk, None)):
            for c in range(NCH):
                wp = wpar % 8; wpar += 1
                em.wait("gp", tok_wstr_rd.get(wpar - 9))
                tok_w = wstream(wstr, src, c, wp, "gw")
                em.wait("pe", tok_w)
                tok_p = None
                for ep in range(NCH):
                    if c == 0:
                        use_bank("pe", ep)
                    tok_p = em.inc("pe", "pe",
                                   MM(BANKS[ep], wstr[:, 1024 * wp + 128 * ep:1024 * wp + 128 * (ep + 1)],
                                      rxb[:, 512 * c:512 * (c + 1)], c == 0, c == NCH - 1))
                tok_wstr_rd[wpar - 1] = tok_p
            for ep in range(NCH):
                if which == "q":
                    dst = qstage[:, 512 * ep:512 * (ep + 1)]
                else:
                    dst = k_sb[:, 2048 * ep + 512 * tt: 2048 * ep + 512 * (tt + 1)]
                em.wait("act", tok_p)
                tok_ev = em.inc("act", "act", ACTI(dst, BANKS[ep], AF.Copy, 0.0))
                if which == "q":
                    tok_qev = tok_ev
                bankfree[ep] = tok_ev
                sq_jobs.append((which, ep, dst, tok_ev))
        em.wait("sync", tok_qev)
        em.wait("sync", tok_qs_wr.get(tt - 1))
        tok_qs_wr[tt] = em.inc("sync", "sqw", DMA(dqs[:, 4096 * tt:4096 * (tt + 1)], qstage[:, :]), 16)
        if tt + 1 < NT:
            em.wait("gp", tok_xn_done.get(tt - 1))
            tok_xload[tt + 1] = xload(tt + 1)
        NPS = 4
        tok_sq_j = {}
        tok_mm_j = {}
        njob = len(sq_jobs)
        for j in range(njob + 2):
            if j < njob:
                which, ep, dst, tok_ev = sq_jobs[j]
                em.wait("dve", tok_ev)
                em.wait("dve", tok_mm_j.get(j - NPS))
                tok_sq_j[j] = em.inc("dve", "dve",
                                     TT(AL.mult, qsqb[:, 512 * (j % NPS):512 * (j % NPS + 1)], dst, dst))
            if 0 <= j - 1 < njob:
                jj = j - 1
                em.wait("pe", tok_sq_j[jj])
                use_bank("pe", jj % NPS)
                tok_mm_j[jj] = em.inc("pe", "pe",
                                      MM(pA[0:1, 512 * (jj % NPS):512 * (jj % NPS) + 512], cb[:, 1:2],
                                         qsqb[:, 512 * (jj % NPS):512 * (jj % NPS + 1)], True, True))
            if 0 <= j - 2 < njob:
                jj = j - 2
                which2, ep2, _, _ = sq_jobs[jj]
                em.wait("dve", tok_mm_j[jj])
                mdst = (mq if which2 == "q" else mk_)[0:1, 4 * ep2 + tt: 4 * ep2 + tt + 1]
                tok_red = em.inc("dve", "dve",
                                 RMAX(mdst, pA[0:1, 512 * (jj % NPS):512 * (jj % NPS) + 512]))
                bankfree[jj % NPS] = tok_red
        if p1cut == 4 and tt == 0:
            bar("c4"); return _realize(nc, em)
        if p1cut == 5 and tt == 1:
            bar("c5"); return _realize(nc, em)

    em.wait("sync", tok_qs_wr[NT - 1])
    em.wait("sync", tok_xns_wr[NT - 1])
    if phases >= 2:
        bar("p1p2")
    if p2cut == 0:
        return _realize(nc, em)

    if phases < 2:
        return _realize(nc, em)
    tok_wcast = None
    for blk in range(8):
        em.inc("gp", "gwc", DMA(dwfcb[128 * blk:128 * (blk + 1), :], dwfc[128 * blk:128 * (blk + 1), :]), 16)
        tok_wcast = em.inc("gp", "gwc", DMA(dwpjb[512 * blk:512 * (blk + 1), :], dwpj[512 * blk:512 * (blk + 1), :]), 16)
    tok_mx = None
    for c in range(NCH):
        em.inc("dve", "dve", RMAX(mqf[0:1, c:c + 1], mq[0:1, 4 * c:4 * c + 4]))
        tok_mx = em.inc("dve", "dve", RMAX(mkf[0:1, c:c + 1], mk_[0:1, 4 * c:4 * c + 4]))
    em.wait("act", tok_mx)
    em.inc("act", "act", ACTI(mqf[0:1, :], mqf[0:1, :], AF.Sqrt, zcol[0:1, 0:1]))
    tok_sq2 = em.inc("act", "act", ACTI(mkf[0:1, :], mkf[0:1, :], AF.Sqrt, zcol[0:1, 0:1]))
    em.wait("dve", tok_sq2)
    em.inc("dve", "dve", TT(AL.mult, ncr[0:1, :], mqf[0:1, :], mkf[0:1, :]))
    em.inc("dve", "dve", TS(ncr2[0:1, :], ncr[0:1, :], -1.004 / 8.0, AL.mult))
    tok_nc = em.inc("dve", "dve", CP(ncrb[0:1, :], ncr2[0:1, :]))
    em.wait("pe", tok_nc)
    tok_ncb = em.inc("pe", "pe", MM(pB[:, 1024:1032], cb[0:1, 130:258], ncrb[0:1, :], True, True))
    em.wait("dve", tok_ncb)
    tok_negc = em.inc("dve", "dve", CP(negc[:, :], pB[:, 1024:1032]))
    em.wait("act", tok_negc)
    bankfree[6] = tok_negc
    if p2cut == 1:
        return _realize(nc, em)
    if debug:
        em.wait("sync", tok_negc)
        em.inc("sync", "sdbg", DMA(dnegc[:, :], negc[:, :]), 16)

    SP = [pA[:, 0:1024], pA[:, 1024:2048]]
    SPB = [(0, 1), (2, 3)]
    OPAR = [(pB[:, 0:512], pB[:, 512:1024], 4, 5),
            (pB[:, 1024:1536], pB[:, 1536:2048], 6, 7)]

    sp_i = 0
    tok_exp = {}
    tok_pv_strip = {}
    tok_pv_pair = {}
    tok_zln = {}
    tok_osb = {}
    tok_score_rd = {}
    tok_qQ = {}
    tok_a_wr = {}
    state = {"wo_ready": None}

    def emit_pair(Q, c, pv_pair):
        nonlocal sp_i
        L = 4 * Q + 3
        em.wait("pe", tok_qQ[Q])
        qs0 = 4096 * (Q % 2)
        for tau in range(L + 1):
            j = max(0, tau - 4 * Q)
            p = sp_i % 2; sp_i += 1
            use_bank("pe", SPB[p][0]); use_bank("pe", SPB[p][1])
            S = SP[p]
            rA = qQ[0:64, qs0 + 512 * c + 128 * j: qs0 + 512 * (c + 1)]
            rB = qQ[64:128, qs0 + 512 * c + 128 * j: qs0 + 512 * (c + 1)]
            lA = k_sb[0:64, 2048 * c + 128 * tau: 2048 * c + 128 * (tau + 1)]
            lB = k_sb[64:128, 2048 * c + 128 * tau: 2048 * c + 128 * (tau + 1)]
            em.op("pe", MM(S[:, 128 * j:512], lA, rA, True, True, (0, 0)))
            tok_s = em.inc("pe", "pe", MM(S[:, 512 + 128 * j:1024], lB, rB, True, True, (64, 0)))
            if tau == L and c == NCH - 1:
                tok_score_rd[Q] = tok_s
            if pv_pair is not None:
                emit_pv_tau(Q, pv_pair, tau)
            gate = tok_s
            if tau >= 4 * Q:
                em.wait("dve", tok_s)
                em.inc("dve", "dve", TT(AL.add, S[:, 128 * j:128 * (j + 1)],
                                        S[:, 128 * j:128 * (j + 1)], cf[:, 128:256]))
                gate = em.inc("dve", "dve", TT(AL.add, S[:, 512 + 128 * j:512 + 128 * (j + 1)],
                                               S[:, 512 + 128 * j:512 + 128 * (j + 1)], cf[:, 128:256]))
            em.wait("act", gate)
            em.wait("act", tok_pv_strip.get((c - 1, tau)))
            em.wait("act", tok_pv_strip.get(("dbg", 0)))
            tok_e = em.inc("act", "act",
                           ACTI(PT[:, 1024 * tau + 128 * j: 1024 * (tau + 1)],
                                S[:, 128 * j:1024], AF.Exp, negc[:, c:c + 1], 0.125))
            tok_exp[(c, tau)] = tok_e
            bankfree[SPB[p][0]] = tok_e
            bankfree[SPB[p][1]] = tok_e

    def emit_pv_tau(Q, c, tau):
        L = 4 * Q + 3
        O1, O2, b1, b2 = OPAR[c % 2]
        j = max(0, tau - 4 * Q)
        if tau == 0:
            use_bank("pe", b1); use_bank("pe", b2)
        em.wait("pe", tok_exp[(c, tau)])
        vo = tau * VROW + c * VBLK
        em.op("pe", MM(O1[:, 128 * j:512], v_ext[:, vo:vo + 128],
                       PT[:, 1024 * tau + 128 * j:1024 * tau + 512],
                       tau == 0, tau == L))
        tok = em.inc("pe", "pe", MM(O2[:, 128 * j:512], v_ext[:, vo + 32:vo + 160],
                                    PT[:, 1024 * tau + 512 + 128 * j:1024 * (tau + 1)],
                                    tau == 0, tau == L))
        tok_pv_strip[(c, tau)] = tok
        if tau == L:
            tok_pv_pair[c] = tok

    def emit_zln(c):
        par = c % 2
        O1, O2, b1, b2 = OPAR[par]
        em.wait("act", tok_pv_pair[c])
        em.inc("act", "act", ACTI(zrow[64:65, 0:512], O1[64:65, 0:512], AF.Ln, zcol[64:65, 0:1]))
        em.inc("act", "act", ACTI(zinv[64:65, 1024 * par:1024 * par + 512], zrow[64:65, 0:512],
                                  AF.Exp, zcol[64:65, 0:1], -1.0))
        em.inc("act", "act", ACTI(zrow[32:33, 512:1024], O2[32:33, 0:512], AF.Ln, zcol[32:33, 0:1]))
        tok_zln[c] = em.inc("act", "act", ACTI(zinv[32:33, 1024 * par + 512:1024 * (par + 1)], zrow[32:33, 512:1024],
                                               AF.Exp, zcol[32:33, 0:1], -1.0))
        em.wait("dve", tok_pv_pair[c])
        em.inc("dve", "dve", CP(osb[0:64, 512 * par:512 * (par + 1)], O1[0:64, :]))
        tok_osb[c] = em.inc("dve", "dve", CP(osb[64:128, 512 * par:512 * (par + 1)], O2[64:128, :]))

    def emit_norm(Q, c):
        par = c % 2
        O1, O2, b1, b2 = OPAR[par]
        em.wait("pe", tok_osb[c])
        em.wait("pe", tok_zln[c])
        em.op("pe", MM(O1[:, 0:512], cb[64:65, 130:258], zinv[64:65, 1024 * par:1024 * par + 512], True, True, (64, 0)))
        tok_bc = em.inc("pe", "pe", MM(O2[:, 0:512], cb[32:33, 130:258], zinv[32:33, 1024 * par + 512:1024 * (par + 1)], True, True, (32, 0)))
        em.wait("dve", tok_bc)
        em.inc("dve", "dve", TT(AL.mult, Ofm[0:64, 512 * c:512 * (c + 1)], osb[0:64, 512 * par:512 * (par + 1)], O1[0:64, 0:512]))
        tok_oe = em.inc("dve", "dve", TT(AL.mult, Ofm[64:128, 512 * c:512 * (c + 1)], osb[64:128, 512 * par:512 * (par + 1)], O2[64:128, 0:512]))
        bankfree[b1] = tok_oe
        bankfree[b2] = tok_oe
        return tok_oe

    def emit_wo(Q):
        em.wait("pe", tok_wo)
        em.wait("pe", state["wo_ready"])
        em.wait("act", tok_a_wr.get(Q - 1))
        tok_ae = None
        woi = 0
        for ep in range(NCH):
            bnk = 6 + (woi % 2); woi += 1
            use_bank("pe", bnk)
            tok_p = None
            for c in range(NCH):
                fn = MM(BANKS[bnk], wo_sb[:, 1024 * c + 128 * ep:1024 * c + 128 * (ep + 1)],
                        Ofm[:, 512 * c:512 * (c + 1)], c == 0, c == NCH - 1)
                if c == NCH - 1:
                    tok_p = em.inc("pe", "pe", fn)
                else:
                    em.op("pe", fn)
            em.wait("act", tok_p)
            tok_ae = em.inc("act", "act", ACTI(Astage[:, 512 * ep:512 * (ep + 1)], BANKS[bnk],
                                               AF.Identity, bo_sb[:, ep:ep + 1]))
            bankfree[bnk] = tok_ae
        em.wait("sync", tok_ae)
        tok_a_wr[Q] = em.inc("sync", "saw", DMA(das[:, 4096 * Q:4096 * (Q + 1)], Astage[:, :]), 16)

    pending_wo = None
    for Q in range(NT):
        em.wait("sync", tok_score_rd.get(Q - 2))
        tok_qQ[Q] = em.inc("sync", f"sqr{Q % 2}",
                           DMA(qQ[:, 4096 * (Q % 2):4096 * (Q % 2 + 1)], dqs[:, 4096 * Q:4096 * (Q + 1)]), 16)
        for c in range(NCH):
            emit_pair(Q, c, c - 1 if c > 0 else None)
            if c == 0 and pending_wo is not None:
                emit_wo(pending_wo)
                pending_wo = None
            if c > 0:
                emit_zln(c - 1)
            if c > 1:
                emit_norm(Q, c - 2)
            if p2cut == 2 and c == 1:
                return _realize(nc, em)
            if p2cut == 3 and c == 5:
                return _realize(nc, em)
            if debug and Q == 1 and c == 0:
                em.wait("sync", tok_exp[(0, 4 * Q + 3)])
                tdbg = em.inc("sync", "sdbg", DMA(dpt[:, 0:4096], PT[:, 0:4096]), 16)
                tok_pv_strip[("dbg", 0)] = tdbg
        for tau in range(4 * Q + 4):
            emit_pv_tau(Q, NCH - 1, tau)
        emit_zln(NCH - 1)
        emit_norm(Q, NCH - 2)
        tok_oe_last = emit_norm(Q, NCH - 1)
        carry = {}
        for tau in range(4 * Q + 4):
            carry[(-1, tau)] = tok_pv_strip.get((NCH - 1, tau))
        tok_pv_strip.clear()
        tok_pv_strip.update(carry)
        if debug:
            em.wait("sync", tok_oe_last)
            em.inc("sync", "sdbg", DMA(dzdbg[:, 16 * Q:16 * (Q + 1)], zrow[:, 0:16]), 16)
            em.inc("sync", "sdbg", DMA(dofm[:, 4096 * Q:4096 * (Q + 1)], Ofm[:, :]), 16)
        state["wo_ready"] = tok_oe_last
        pending_wo = Q
        if p2cut == 35:
            return _realize(nc, em)
        if p2cut == 4:
            emit_wo(Q)
            em.wait("sync", tok_a_wr[Q])
            return _realize(nc, em)

    if pending_wo is not None:
        emit_wo(pending_wo)
        pending_wo = None

    if debug:
        for c in range(NCH):
            em.inc("sync", "sdbg2", DMA(dkdbg[:, 2048 * c:2048 * (c + 1)], k_sb[:, 2048 * c:2048 * (c + 1)]), 16)
        for s in range(NKT):
            em.inc("sync", "sdbg2", DMA(dvdbg[:, VROW * s:VROW * (s + 1)], v_ext[:, VROW * s:VROW * (s + 1)]), 16)
        em.wait("sync", ("sdbg2", 16 * (NCH + NKT)))

    em.wait("sync", tok_a_wr[NT - 1])
    if phases < 3:
        return _realize(nc, em)
    bar("p2p3")

    em.wait("sync", tok_wcast)
    em.wait("gp", tok_wcast)
    tok_pj = None
    pj_next = [0]

    def pj_feed(n):
        nonlocal tok_pj
        for _ in range(n):
            fc = pj_next[0]
            if fc >= NFC:
                return
            pj_next[0] += 1
            tok_pj = em.inc("sync", "gpj", DMA(wpj_sb[:, 1024 * fc:1024 * (fc + 1)],
                                               dwpjb[128 * fc:128 * (fc + 1), :]), 16)

    HB = [BANKS[0], BANKS[1]]
    YB = [BANKS[4], BANKS[5]]
    hb_i = 0
    yb_i = 0
    wfc_hmm = []
    tok_hmm = {}
    tok_xn3 = {}
    tok_ymm_all = {}
    tok_out_dma = {}
    tok_sq_dve = {}
    tok_yread = {}

    def xn3_load(t):
        em.wait("sync", tok_hmm.get((t - 2, 7)))
        tok_xn3[t] = em.inc("sync", f"snr{t % 2}",
                            DMA(xn3[:, 4096 * (t % 2):4096 * (t % 2 + 1)],
                                dxns[:, 4096 * t:4096 * (t + 1)]), 16)

    xn3_load(0)
    for tt in range(NT):
        for f in range(8):
            gi = tt * 8 + f
            par = gi % 2
            if len(wfc_hmm) >= 2:
                em.wait("gp", wfc_hmm[-2])
            tok_wfc = None
            for c in range(NCH):
                tok_wfc = em.inc("gp", f"gfc{par}",
                                 DMA(wfc_sb[:, 4096 * par + 512 * c:4096 * par + 512 * (c + 1)],
                                     dwfcb[128 * c:128 * (c + 1), 512 * f:512 * (f + 1)]), 16)
            pj_feed(4)
            em.wait("pe", tok_wfc)
            em.wait("pe", tok_xn3[tt])
            em.wait("pe", tok_ymm_all.get(tt - 2))
            tok_h_last = None
            tok_relu = None
            for fs in range(4):
                bank = hb_i % 2; hb_i += 1
                use_bank("pe", bank)
                tok_h = None
                for c in range(NCH):
                    fn = MM(HB[bank], wfc_sb[:, 4096 * par + 512 * c + 128 * fs:4096 * par + 512 * c + 128 * (fs + 1)],
                            xn3[:, 4096 * (tt % 2) + 512 * c:4096 * (tt % 2) + 512 * (c + 1)],
                            c == 0, c == NCH - 1)
                    if c == NCH - 1:
                        tok_h = em.inc("pe", "pe", fn)
                    else:
                        em.op("pe", fn)
                tok_h_last = tok_h
                em.wait("act", tok_h)
                em.wait("act", tok_sq_dve.get((tt, f - 1)) if f > 0 else tok_sq_dve.get((tt - 1, 7)))
                tok_relu = em.inc("act", "act", ACTI(rsb[:, 512 * fs:512 * (fs + 1)], HB[bank], AF.Relu, 0.0))
                bankfree[bank] = tok_relu
            tok_hmm[(tt, f)] = tok_h_last
            wfc_hmm.append(tok_h_last)
            em.wait("dve", tok_relu)
            tok_sq_dve[(tt, f)] = em.inc("dve", "dve",
                                         TT(AL.mult, h_sb[:, 16384 * (tt % 2) + 2048 * f:16384 * (tt % 2) + 2048 * (f + 1)],
                                            rsb[:, :], rsb[:, :]))
        tok_hsq = tok_sq_dve[(tt, 7)]
        if tt + 1 < NT:
            xn3_load(tt + 1)
        tok_y_last = None
        for ep in range(NCH):
            ei = tt * 8 + ep
            bank = yb_i % 2; yb_i += 1
            use_bank("pe", 4 + bank)
            em.wait("pe", tok_hsq)
            pj_feed(NFC)
            em.wait("pe", tok_pj)
            tok_y = None
            for fc in range(NFC):
                fn = MM(YB[bank], wpj_sb[:, (fc * 8 + ep) * 128:(fc * 8 + ep) * 128 + 128],
                        h_sb[:, 16384 * (tt % 2) + 512 * fc:16384 * (tt % 2) + 512 * (fc + 1)], fc == 0, fc == NFC - 1)
                if fc == NFC - 1:
                    tok_y = em.inc("pe", "pe", fn)
                else:
                    em.op("pe", fn)
            tok_y_last = tok_y
            em.wait("act", tok_y)
            em.wait("act", tok_yread.get(ei - 2))
            tok_ye = em.inc("act", "act", ACTI(y_sb[:, 512 * (ei % 2):512 * (ei % 2 + 1)], YB[bank], AF.Copy, 0.0))
            bankfree[4 + bank] = tok_ye
            pxa = ei % 2
            em.wait("sync", tok_out_dma.get(ei - 2))
            em.inc("sync", f"sxr{pxa}", DMA(xin[:, 512 * pxa:512 * (pxa + 1)],
                                            dx[128 * ep:128 * (ep + 1), 512 * tt:512 * (tt + 1)]), 16)
            tok_xa = em.inc("sync", f"sxr{pxa}", DMA(ain[:, 512 * pxa:512 * (pxa + 1)],
                                                     das[:, 4096 * tt + 512 * ep:4096 * tt + 512 * (ep + 1)]), 16)
            em.wait("dve", tok_ye)
            yv = y_sb[:, 512 * (ei % 2):512 * (ei % 2 + 1)]
            em.inc("dve", "dve", TS(accs[:, :], yv, 1.0, AL.is_gt))
            em.inc("dve", "dve", STT(mem[:, :], yv, 0.9, yv, AL.mult, AL.add))
            em.inc("dve", "dve", TT(AL.subtract, mem[:, :], mem[:, :], accs[:, :]))
            tok_yr = None
            for _ in range(3):
                em.inc("dve", "dve", TS(spk[:, :], mem[:, :], 1.0, AL.is_gt))
                em.inc("dve", "dve", TT(AL.add, accs[:, :], accs[:, :], spk[:, :]))
                tok_yr = em.inc("dve", "dve", STT(mem[:, :], mem[:, :], 0.9, yv, AL.mult, AL.add))
                em.inc("dve", "dve", TT(AL.subtract, mem[:, :], mem[:, :], spk[:, :]))
            tok_yread[ei] = tok_yr
            em.inc("dve", "dve", TS(spk[:, :], mem[:, :], 1.0, AL.is_gt))
            em.inc("dve", "dve", TT(AL.add, accs[:, :], accs[:, :], spk[:, :]))
            em.wait("dve", tok_xa)
            em.wait("dve", tok_out_dma.get(ei - 2))
            em.inc("dve", "dve", TT(AL.add, outb[:, 512 * pxa:512 * (pxa + 1)],
                                    xin[:, 512 * pxa:512 * (pxa + 1)], ain[:, 512 * pxa:512 * (pxa + 1)]))
            tok_fin = em.inc("dve", "dve", STT(outb[:, 512 * pxa:512 * (pxa + 1)], accs[:, :], 0.2,
                                               outb[:, 512 * pxa:512 * (pxa + 1)], AL.mult, AL.add))
            em.wait("sync", tok_fin)
            tok_out_dma[ei] = em.inc("sync", "sow",
                                     DMA(dout[128 * ep:128 * (ep + 1), 512 * tt:512 * (tt + 1)],
                                         outb[:, 512 * pxa:512 * (pxa + 1)]), 16)
        tok_ymm_all[tt] = tok_y_last

    em.wait("sync", tok_out_dma[NT * NCH - 1])
    return _realize(nc, em)


def _realize(nc, em):
    semnames = sorted(em.cnt.keys())
    with contextlib.ExitStack() as st:
        sems = {n: st.enter_context(nc.semaphore(n)) for n in semnames}
        with nc.Block() as block:
            block.tensor(lambda En: em.run("pe", En, sems))
            block.scalar(lambda En: em.run("act", En, sems))
            block.vector(lambda En: em.run("dve", En, sems))
            block.sync(lambda En: em.run("sync", En, sems))
            block.gpsimd(lambda En: em.run("gp", En, sems))
    return nc


def _host_inputs(L0_attn_Wq, L0_attn_Wk, L0_attn_Wv, L0_attn_Wo, L0_attn_bo,
                 L0_ffn_W_fc, L0_ffn_W_proj):
    C = lambda a: np.ascontiguousarray(a)
    shared = {
        "wq": C(np.asarray(L0_attn_Wq).T.astype(np.float32)),
        "wk": C(np.asarray(L0_attn_Wk).T.astype(np.float32)),
        "wv": C(np.asarray(L0_attn_Wv).T.astype(np.float32)),
        "wo": C(np.asarray(L0_attn_Wo).T.astype(np.float32)),
        "wfc": C(np.asarray(L0_ffn_W_fc).T.astype(np.float32)),
        "wpj": C(np.asarray(L0_ffn_W_proj).T.astype(np.float32)),
        "bo": C(np.asarray(L0_attn_bo).astype(np.float32).reshape(8, 128).T),
    }
    try:
        import jax
        import jax.numpy as jnp
        with jax.default_device(jax.devices("cpu")[0]):
            thetas = 10000.0 ** (-jnp.arange(0, E, 2, dtype=jnp.float32) / E)
            thetas = jnp.repeat(thetas, 2)
            ang = jnp.arange(T, dtype=jnp.float32)[:, None] * thetas[None, :]
            sin_t = np.asarray(jnp.sin(ang))
            cos_t = np.asarray(jnp.cos(ang))
    except Exception:
        thetas = (10000.0 ** (-np.arange(0, E, 2, dtype=np.float32) / np.float32(E))).astype(np.float32)
        thetas = np.repeat(thetas, 2)
        ang = (np.arange(T, dtype=np.float32)[:, None] * thetas[None, :]).astype(np.float32)
        sin_t = np.sin(ang).astype(np.float32)
        cos_t = np.cos(ang).astype(np.float32)
    shared["cosT"] = C(cos_t.T)
    sfold = np.ascontiguousarray(sin_t.T)
    sfold[1::2, :] *= -1.0
    shared["sinT"] = C(sfold)
    cfh = np.zeros((128, 288), np.float32)
    cfh[:, 0:128] = 1.0
    i = np.arange(128)
    cfh[:, 128:256] = np.where(i[None, :] >= i[:, None], 0.0, NEG).astype(np.float32)
    cfh[:, 256] = 1e-5
    shared["cf"] = cfh
    cbh = np.zeros((128, 392), np.float32)
    cbh[:, 0] = 1.0 / 1024.0
    cbh[:, 1] = 1.0
    pit = np.zeros((128, 128), np.float32)
    for ii in range(64):
        pit[2 * ii + 1, 2 * ii] = -1.0
        pit[2 * ii, 2 * ii + 1] = 1.0
    cbh[:, 2:130] = pit
    cbh[:, 130:258] = 1.0
    cbh[:, 258:386] = 1.0 / 1024.0
    shared["cb"] = cbh.astype(ml_dtypes.bfloat16)
    return shared


def kernel(L0_attn_input, L0_attn_Wq, L0_attn_Wk, L0_attn_Wv, L0_attn_Wo,
           L0_attn_bo, L0_ffn_W_fc, L0_ffn_W_proj):
    x = np.asarray(L0_attn_input)
    assert x.shape == (8, T, E), x.shape
    shared = _host_inputs(L0_attn_Wq, L0_attn_Wk, L0_attn_Wv, L0_attn_Wo,
                          L0_attn_bo, L0_ffn_W_fc, L0_ffn_W_proj)
    in_maps = []
    for i in range(8):
        d = dict(shared)
        d["x"] = np.ascontiguousarray(x[i].T.astype(np.float32))
        in_maps.append(d)
    nc = build(debug=False)
    res = run_bass_kernel_spmd(nc, in_maps, core_ids=list(range(8)))
    out = np.stack([np.asarray(r["out"]).T for r in res.results], axis=0)
    return out.astype(np.float32)
